# version 04
# baseline (speedup 1.0000x reference)
import math
import numpy as np
import ml_dtypes
import concourse.bass as bass
import concourse.mybir as mybir
from concourse.bass_utils import run_bass_kernel_spmd

F32 = mybir.dt.float32
BF16 = mybir.dt.bfloat16
I32 = mybir.dt.int32
AF = mybir.ActivationFunctionType
ALU = mybir.AluOpType
AX = mybir.AxisListType
BF = ml_dtypes.bfloat16

D = 2048
T = 2048
NCH = D // 128
SL = 512
NSL = T // SL
NHEADS = 16
PI = math.pi

ENG = ("pe", "act", "dve", "pool", "sp")
FUSED = True



class S:
    __slots__ = ("b", "ap", "p0", "p1", "c0", "c1")

    def __init__(self, b, ap, p0, p1, c0, c1):
        self.b, self.ap = b, ap
        self.p0, self.p1, self.c0, self.c1 = p0, p1, c0, c1


class TBuf:

    def __init__(self, h, name, parts, cols):
        self.h, self.name = h, name
        self.parts, self.cols = parts, cols
        self.writes = []
        self.reads = {}
        ALL_BUFS.append(self)

    def __getitem__(self, key):
        ps, cs = key
        p0 = ps.start or 0
        p1 = ps.stop if ps.stop is not None else self.parts
        c0 = cs.start or 0
        c1 = cs.stop if cs.stop is not None else self.cols
        return S(self, self.h[p0:p1, c0:c1], p0, p1, c0, c1)

    def view(self, p0, p1, c0, c1, ap):
        return S(self, ap, p0, p1, c0, c1)


class DBuf:

    def __init__(self, t, name, nelem):
        self.t, self.name, self.nelem = t, name, nelem
        self.writes = []
        self.reads = {}
        ALL_BUFS.append(self)

    def ap(self, offset, pattern):
        hi = offset + 1
        for st, n in pattern:
            hi += (n - 1) * st
        return S(self, bass.AP(self.t, offset, pattern), 0, 1, offset, hi)

    def slab(self, rowlen, c, t0, n, nrows=128):
        return self.ap(c * 128 * rowlen + t0, [[rowlen, nrows], [1, n]])

    def whole(self):
        return self.ap(0, [[1, self.nelem]])


def _ovl(a, b):
    return not (a[1] <= b[0] or b[1] <= a[0] or a[3] <= b[2] or b[3] <= a[2])


ALL_BUFS = []


class Pr:

    def __init__(self, nc):
        self.nc = nc
        self.ops = {e: [] for e in ENG}
        self.last_dma = {}
        self.last_op = {}
        self.fence_toks = []

    def fence(self):
        self.fence_toks = [(e, i, d) for (e, d), i in self.last_op.items()]
        for b in ALL_BUFS:
            b.writes = []
            b.reads = {}

    def _emit(self, eng, fn, reads, writes, dma=False, extra_deps=()):
        deps = set(extra_deps)
        deps.update(self.fence_toks)
        for s in reads:
            if s is None or isinstance(s, float):
                continue
            r = (s.p0, s.p1, s.c0, s.c1)
            for w in s.b.writes:
                if _ovl(r, w[:4]):
                    deps.add(w[4])
        for s in writes:
            r = (s.p0, s.p1, s.c0, s.c1)
            for w in s.b.writes:
                if _ovl(r, w[:4]):
                    deps.add(w[4])
            for k, idx in s.b.reads.items():
                if _ovl(r, k[:4]):
                    deps.add((k[4], idx, k[5]))
        idx = len(self.ops[eng])
        tok = (eng, idx, dma)
        self.ops[eng].append({"fn": fn, "deps": deps, "dma": dma})
        self.last_op[(eng, dma)] = idx
        for s in reads:
            if s is None or isinstance(s, float):
                continue
            s.b.reads[(s.p0, s.p1, s.c0, s.c1, eng, dma)] = idx
        for s in writes:
            r = (s.p0, s.p1, s.c0, s.c1)
            s.b.writes = [w for w in s.b.writes if not _cover(r, w[:4])]
            rd = s.b.reads
            for k in [k for k in rd if _cover(r, k[:4])]:
                del rd[k]
            s.b.writes.append([s.p0, s.p1, s.c0, s.c1, tok])
        return tok

    def dma(self, eng, dst, src, **kw):
        extra = [self.last_dma[eng]] if eng in self.last_dma else []
        tok = self._emit(eng, lambda e, dst=dst, src=src, kw=kw:
                         e.dma_start(dst.ap, src.ap, **kw),
                         [src], [dst], dma=True, extra_deps=extra)
        self.last_dma[eng] = tok
        return tok

    def mm(self, ps, lhsT, rhs, start, stop):
        return self._emit("pe", lambda e, ps=ps, l=lhsT, r=rhs, a=start, b=stop:
                          e.matmul(ps.ap, l.ap, r.ap, start=a, stop=b),
                          [lhsT, rhs, ps], [ps])

    def tp(self, ps, in_, ident):
        return self._emit("pe", lambda e, ps=ps, i=in_, d=ident:
                          e.transpose(ps.ap, i.ap, d.ap), [in_, ident], [ps])

    def act(self, out, in_, func, bias=0.0, scale=1.0, accum=None, eng="act"):
        wr = [out] + ([accum] if accum is not None else [])
        return self._emit(eng, lambda e, o=out, i=in_, f=func, b=bias, s=scale,
                          a=accum: e.activation(
                              o.ap, i.ap, f,
                              bias=(b.ap if isinstance(b, S) else b),
                              scale=(s.ap if isinstance(s, S) else s),
                              accum_out=(a.ap if a is not None else None)),
                          [in_, bias, scale], wr)

    def tt(self, out, a, b, op, eng="dve"):
        return self._emit(eng, lambda e, o=out, x=a, y=b, p=op:
                          e.tensor_tensor(o.ap, x.ap, y.ap, op=p),
                          [a, b], [out])

    def ts(self, out, in_, s1, s2, op0, op1, eng="dve"):
        return self._emit(eng, lambda e, o=out, i=in_, a=s1, b=s2, p0=op0, p1=op1:
                          e.tensor_scalar(o.ap, i.ap,
                                          a.ap if isinstance(a, S) else a,
                                          b.ap if isinstance(b, S) else b,
                                          op0=p0, op1=p1),
                          [in_, s1, s2], [out])

    def tss(self, out, in_, sc, op, eng="dve"):
        return self._emit(eng, lambda e, o=out, i=in_, s=sc, p=op:
                          e.tensor_single_scalar(
                              o.ap, i.ap, s.ap if isinstance(s, S) else s, op=p),
                          [in_, sc], [out])

    def stt(self, out, in0, sc, in1, op0, op1, eng="dve"):
        return self._emit(eng, lambda e, o=out, i=in0, s=sc, j=in1, p0=op0, p1=op1:
                          e.scalar_tensor_tensor(
                              o.ap, i.ap, s.ap if isinstance(s, S) else s, j.ap,
                              op0=p0, op1=p1),
                          [in0, sc, in1], [out])

    def red(self, out, in_, op, eng="dve"):
        return self._emit(eng, lambda e, o=out, i=in_, p=op:
                          e.tensor_reduce(o.ap, i.ap, axis=AX.X, op=p),
                          [in_], [out])

    def copy(self, out, in_, eng="dve"):
        if eng == "act":
            return self._emit(eng, lambda e, o=out, i=in_: e.copy(o.ap, i.ap),
                              [in_], [out])
        return self._emit(eng, lambda e, o=out, i=in_: e.tensor_copy(o.ap, i.ap),
                          [in_], [out])

    def memset(self, out, v, eng="dve"):
        return self._emit(eng, lambda e, o=out, v=v: e.memset(o.ap, v), [], [out])

    def recip(self, out, in_):
        return self._emit("dve", lambda e, o=out, i=in_:
                          e.reciprocal(o.ap, i.ap), [in_], [out])

    def finish(self, block, sems):
        needed = {e: set() for e in ENG}
        for e in ENG:
            for op in self.ops[e]:
                for (de, di, dd) in op["deps"]:
                    if de == e == "pe":
                        continue
                    needed[de].add(di)
        for e in ENG:
            n = len(self.ops[e])
            if n:
                needed[e].add(n - 1)
            for i, op in enumerate(self.ops[e]):
                if op["dma"]:
                    needed[e].add(i)
        incpos = {}
        finals = {}
        for e in ENG:
            srt = sorted(needed[e])
            pos = {idx: k + 1 for k, idx in enumerate(srt)}
            incpos[e] = pos
            amt = 16 if any(self.ops[e][i]["dma"] for i in srt) else 1
            finals[e] = pos

        def value_of(de, di, dd):
            cnt = 0
            for j in sorted(needed[de]):
                if j <= di and self.ops[de][j]["dma"] == dd:
                    cnt += 1
            return cnt * (16 if dd else 1)

        inc_sorted = {}
        for e in ENG:
            inc_sorted[(e, False)] = sorted(
                i for i in needed[e] if not self.ops[e][i]["dma"])
            inc_sorted[(e, True)] = sorted(
                i for i in needed[e] if self.ops[e][i]["dma"])
        import bisect

        def val(de, di, dd):
            lst = inc_sorted[(de, dd)]
            return (bisect.bisect_right(lst, di)) * (16 if dd else 1)

        runners = {"pe": block.tensor, "dve": block.vector, "act": block.scalar,
                   "pool": block.gpsimd, "sp": block.sync}

        def make(e):
            ops = self.ops[e]
            my_inc_c = set(inc_sorted[(e, False)])
            my_inc_d = set(inc_sorted[(e, True)])

            def run(engine):
                water = {}
                cnt_c = cnt_d = 0
                for i, op in enumerate(ops):
                    for (de, di, dd) in sorted(op["deps"]):
                        if de == e == "pe" and not dd:
                            continue
                        v = val(de, di, dd)
                        if v <= 0:
                            continue
                        key = (de, dd)
                        if water.get(key, 0) >= v:
                            continue
                        engine.wait_ge(sems[key], v)
                        water[key] = v
                    instr = op["fn"](engine)
                    if op["dma"]:
                        if i in my_inc_d:
                            cnt_d += 1
                            instr.then_inc(sems[(e, True)], 16)
                    else:
                        if i in my_inc_c:
                            cnt_c += 1
                            instr.then_inc(sems[(e, False)], 1)
                if e == "sp":
                    for (de, dd), lst in inc_sorted.items():
                        if not lst:
                            continue
                        v = len(lst) * (16 if dd else 1)
                        if water.get((de, dd), 0) < v:
                            engine.wait_ge(sems[(de, dd)], v)
            return run

        for e in ENG:
            runners[e](make(e))


def _cover(r, w):
    return r[0] <= w[0] and r[1] >= w[1] and r[2] <= w[2] and r[3] >= w[3]



class Prog:

    def __init__(self, name):
        self.nc = bass.Bass("TRN2", target_bir_lowering=False)
        self.name = name
        self.d = {}
        self.ctx = None

    def din(self, name, shape, dt=F32):
        if name in self.d:
            return self.d[name]
        t = self.nc.dram_tensor(name, list(shape), dt, kind="ExternalInput")
        n = 1
        for s in shape:
            n *= s
        self.d[name] = DBuf(t, name, n)
        return self.d[name]

    def dout(self, name, shape, dt=F32):
        t = self.nc.dram_tensor(name, list(shape), dt, kind="ExternalOutput")
        n = 1
        for s in shape:
            n *= s
        return DBuf(t, name, n)

    def dscratch(self, name, shape, dt=F32):
        key = "scr_" + name
        if key in self.d:
            return self.d[key]
        t = self.nc.dram_tensor(name, list(shape), dt, kind="Internal")
        n = 1
        for s in shape:
            n *= s
        self.d[key] = DBuf(t, name, n)
        return self.d[key]


_UID = [0]


def sbuf(pr_ctx, nc, st, name, parts, cols, dt):
    _UID[0] += 1
    name = f"{name}_{_UID[0]}"
    h = st.enter_context(nc.sbuf_tensor(name, [parts, cols], dt))
    return TBuf(h, name, parts, cols)


def psum(nc, st, name, parts, cols, dt):
    h = st.enter_context(nc.psum_tensor(name, [parts, cols], dt))
    return TBuf(h, name, parts, cols)


def load_pcols(p, pr, dram_vec, dst, base, n):
    src = dram_vec.ap(0, [[1, 128], [128, n]])
    pr.dma("sp", dst.view(0, 128, base, base + n, dst.h[:, base:base + n]),
           src, allow_slow_non_contiguous=True)


SLN = 256


def emit_ln(pr, c, src, dst, eps, g=None, b=None, dst_dram=None,
            src_dram_f32=None):
    XF, SQ, RW, PA, PB, PC, PD, XT = (c["XF"], c["SQ"], c["RW"], c["psA"],
                                      c["psB"], c["psC"], c["psD"], c["XT"])
    ones_f = c["ones_f"]
    ones_bf = c["ones_bf"]
    ones_row = c["ones_row"]
    for ts in range(T // SLN):
        t0 = ts * SLN
        f32src = src_dram_f32 is not None
        if f32src:
            for ch in range(NCH):
                pr.dma("sp", XF[:, ch * SLN:(ch + 1) * SLN],
                       src_dram_f32.slab(T, ch, t0, SLN))

        def chunk(ch):
            if f32src:
                return XF[:, ch * SLN:(ch + 1) * SLN]
            return src[:, ch * T + t0:ch * T + t0 + SLN]

        for ch in range(NCH):
            xv = chunk(ch)
            sq = SQ[:, (ch % 2) * SLN:(ch % 2 + 1) * SLN]
            pr.act(sq, xv, AF.Square)
            pr.mm(PA[0:1, 0:SLN], (ones_f if f32src else ones_bf), xv,
                  start=(ch == 0), stop=(ch == NCH - 1))
            pr.mm(PB[0:1, 0:SLN], ones_f, sq,
                  start=(ch == 0), stop=(ch == NCH - 1))
        mean = RW[0:1, 0:SLN]
        var = RW[0:1, SLN:2 * SLN]
        rstd = RW[0:1, 2 * SLN:3 * SLN]
        nm = RW[0:1, 3 * SLN:4 * SLN]
        pr.tss(mean, PA[0:1, 0:SLN], 1.0 / D, ALU.mult)
        pr.act(var, mean, AF.Square)
        pr.stt(var, PB[0:1, 0:SLN], 1.0 / D, var, ALU.mult, ALU.subtract)
        pr.act(rstd, var, AF.Sqrt, bias=ccol(pr, c, float(eps), rows=1))
        pr.recip(rstd, rstd)
        pr.tss(nm, mean, -1.0, ALU.mult)
        pr.mm(PC[:, 0:SLN], ones_row, nm, start=True, stop=True)
        pr.mm(PD[:, 0:SLN], ones_row, rstd, start=True, stop=True)
        for ch in range(NCH):
            xv = chunk(ch)
            t1 = XT[:, (ch % 2) * SLN:(ch % 2 + 1) * SLN]
            pr.tt(t1, xv, PC[:, 0:SLN], ALU.add)
            if dst_dram is not None:
                t2 = XT[:, (2 + ch % 2) * SLN:(3 + ch % 2) * SLN]
                pr.tt(t2, t1, PD[:, 0:SLN], ALU.mult)
                if g is not None:
                    gb_, gbase = g
                    bb_, bbase = b
                    t3 = XT[:, (ch % 2) * SLN:(ch % 2 + 1) * SLN]
                    pr.ts(t3, t2, gb_[:, gbase + ch:gbase + ch + 1],
                          bb_[:, bbase + ch:bbase + ch + 1], ALU.mult, ALU.add)
                    t2 = t3
                pr.dma("pool", dst_dram.slab(T, ch, t0, SLN), t2)
                if dst is not None:
                    pr.copy(dst[:, ch * T + t0:ch * T + t0 + SLN], t2)
            else:
                ov = dst[:, ch * T + t0:ch * T + t0 + SLN]
                if g is None:
                    pr.tt(ov, t1, PD[:, 0:SLN], ALU.mult)
                else:
                    t2 = XT[:, (2 + ch % 2) * SLN:(3 + ch % 2) * SLN]
                    pr.tt(t2, t1, PD[:, 0:SLN], ALU.mult)
                    gb_, gbase = g
                    bb_, bbase = b
                    pr.ts(ov, t2, gb_[:, gbase + ch:gbase + ch + 1],
                          bb_[:, bbase + ch:bbase + ch + 1], ALU.mult, ALU.add)


def wload_f32(pr, c, wdram, K, M, o0, ow, slot, k0=0, kn=None):
    if kn is None:
        kn = K // 128
    WF, WB = c["WF"], c["WB"]
    nel = kn * ow
    src = wdram.ap(k0 * 128 * M + o0, [[M, 128], [128 * M, kn], [1, ow]])
    dstf = WF.view(0, 128, 0, nel, WF.h[:, 0:nel].rearrange(
        "p (c n) -> p c n", n=ow))
    pr.dma("sp", dstf, src)
    pr.copy(WB.view(0, 128, slot, slot + nel, WB.h[:, slot:slot + nel]),
            WF[:, 0:nel], eng="act")
    return [WB[:, slot + k * ow:slot + (k + 1) * ow] for k in range(kn)]


def wload_i32(pr, c, wdram, K, M, o0, ow, slot):
    kn = K // 128
    WI, WB = c["WI"], c["WB"]
    nel = kn * ow
    src = wdram.ap(o0, [[M, 128], [128 * M, kn], [1, ow]])
    dsti = WI.view(0, 128, 0, nel, WI.h[:, 0:nel].rearrange(
        "p (c n) -> p c n", n=ow))
    pr.dma("sp", dsti, src)
    pr.tss(WB.view(0, 128, slot, slot + nel, WB.h[:, slot:slot + nel]),
           WI[:, 0:nel], -128.0, ALU.add)
    return [WB[:, slot + k * ow:slot + (k + 1) * ow] for k in range(kn)]


def common_consts(pr, c, nc, st):
    c["onesbuf"] = sbuf(pr, nc, st, "onesbuf", 128, 4, F32)
    c["onesbf"] = sbuf(pr, nc, st, "onesbfb", 128, 4, BF16)
    pr.memset(c["onesbuf"][:, 0:4], 1.0)
    pr.memset(c["onesbf"][:, 0:4], 1.0)
    c["ones_f"] = c["onesbuf"][:, 0:1]
    c["ones_bf"] = c["onesbf"][:, 0:1]
    c["rowbuf"] = sbuf(pr, nc, st, "rowbuf", 1, 128, F32)
    pr.memset(c["rowbuf"][0:1, 0:128], 1.0)
    c["ones_row"] = c["rowbuf"][0:1, 0:128]
    c["cpool"] = sbuf(pr, nc, st, "cpool", 128, 8, F32)
    c["cmap"] = {}


def ccol(pr, c, v, rows=128):
    if v not in c["cmap"]:
        i = len(c["cmap"])
        assert i < 8
        pr.memset(c["cpool"][:, i:i + 1], float(v))
        c["cmap"][v] = i
    i = c["cmap"][v]
    return c["cpool"][0:rows, i:i + 1]


def ln_bufs(pr, c, nc, st):
    c["XF"] = sbuf(pr, nc, st, "XFb", 128, NCH * SLN, F32)
    c["SQ"] = sbuf(pr, nc, st, "SQb", 128, 2 * SLN, F32)
    c["RW"] = sbuf(pr, nc, st, "RWb", 128, 4 * SLN, F32)
    c["XT"] = sbuf(pr, nc, st, "XTb", 128, 4 * SLN, F32)
    c["psA"] = psum(nc, st, "psA", 128, SL, F32)
    c["psB"] = psum(nc, st, "psB", 128, SL, F32)
    c["psC"] = psum(nc, st, "psC", 128, SL, F32)
    c["psD"] = psum(nc, st, "psD", 128, SL, F32)



def emit_l1ffn(pr, c, p, nc, rin, rout):
    pr.fence()
    wup = p.din("wupT", (D, D))
    wdn = p.din("wdnT", (D, D))
    wex = p.din("wex", (D, D))
    import contextlib
    with contextlib.ExitStack() as st:
        c["WF"] = sbuf(pr, nc, st, "WFb", 128, NCH * 128, F32)
        c["WB"] = sbuf(pr, nc, st, "WBb", 128, 2 * NCH * 128, BF16)
        XN = sbuf(pr, nc, st, "XNb", 128, NCH * T, BF16)
        H = sbuf(pr, nc, st, "Hb", 128, NCH * T, BF16)
        OF = sbuf(pr, nc, st, "OFb", 128, 2 * SL, F32)
        RS = sbuf(pr, nc, st, "RSb", 128, 2 * SL, F32)
        P0 = c["P0"]
        P1 = c["P1"]
        PT = c["PT"]
        print(self_name := "l1ffn", "sbuf remaining:", nc.sbuf_bytes_remaining)

        emit_ln(pr, c, None, XN, 1e-5, src_dram_f32=rin)
        for oc in range(NCH):
            wb = wload_f32(pr, c, wup, D, D, oc * 128, 128,
                           (oc % 2) * NCH * 128)
            for tb in range(NSL):
                ps = P0 if tb % 2 == 0 else P1
                for kc in range(NCH):
                    pr.mm(ps[:, 0:SL], wb[kc],
                          XN[:, kc * T + tb * SL:kc * T + (tb + 1) * SL],
                          start=(kc == 0), stop=(kc == NCH - 1))
                pr.act(H[:, oc * T + tb * SL:oc * T + (tb + 1) * SL],
                       ps[:, 0:SL], AF.Relu)
        for oc in range(NCH):
            wb1 = wload_f32(pr, c, wdn, D, D, oc * 128, 128, 0)
            wb2 = wload_f32(pr, c, wex, D, D, oc * 128, 128, NCH * 128)
            for tb in range(NSL):
                ps = P0 if tb % 2 == 0 else P1
                for kc in range(NCH):
                    pr.mm(ps[:, 0:SL], wb1[kc],
                          H[:, kc * T + tb * SL:kc * T + (tb + 1) * SL],
                          start=(kc == 0), stop=False)
                for kc in range(NCH):
                    pr.mm(ps[:, 0:SL], wb2[kc],
                          XN[:, kc * T + tb * SL:kc * T + (tb + 1) * SL],
                          start=False, stop=(kc == NCH - 1))
                i = (oc * NSL + tb) % 2
                rs = RS[:, i * SL:(i + 1) * SL]
                of = OF[:, i * SL:(i + 1) * SL]
                pr.dma("sp", rs, rin.slab(T, oc, tb * SL, SL))
                pr.tt(of, ps[:, 0:SL], rs, ALU.add)
                pr.dma("pool", rout.slab(T, oc, tb * SL, SL), of)



HID = 5632
NHC = HID // 128
SLB = 256


def emit_l0ffn(pr, c, p, nc, rin, rout):
    pr.fence()
    wgT = p.din("wgT", (D, HID), I32)
    wuT = p.din("wuT", (D, HID), I32)
    wdT = p.din("wdT", (HID, D), I32)
    sg = p.din("sg", (HID,))
    su = p.din("su", (HID,))
    sd = p.din("sd", (D,))
    HS = p.dscratch("HS", (HID, T), BF16)
    import contextlib
    with contextlib.ExitStack() as st:
        c["WI"] = sbuf(pr, nc, st, "WIb", 128, NHC * 128, I32)
        c["WB"] = sbuf(pr, nc, st, "WBb", 128, NHC * 128, BF16)
        XN = sbuf(pr, nc, st, "XNb", 128, NCH * T, BF16)
        HSS = sbuf(pr, nc, st, "HSSb", 128, NHC * SLB, BF16)
        HT = sbuf(pr, nc, st, "HTb", 128, 6 * SL, F32)
        HB = sbuf(pr, nc, st, "HBb", 128, 2 * SL, BF16)
        OF = sbuf(pr, nc, st, "OFb", 128, 2 * SLB, F32)
        RS = sbuf(pr, nc, st, "RSb", 128, 2 * SLB, F32)
        PCOL = sbuf(pr, nc, st, "PCOLb", 128, NHC * 2 + NCH, F32)
        P0 = c["P0"]
        P1 = c["P1"]
        PT = c["PT"]
        print("l0ffn sbuf remaining:", nc.sbuf_bytes_remaining)

        load_pcols(p, pr, sg, PCOL, 0, NHC)
        load_pcols(p, pr, su, PCOL, NHC, NHC)
        load_pcols(p, pr, sd, PCOL, 2 * NHC, NCH)
        emit_ln(pr, c, None, XN, 1e-5, src_dram_f32=rin)
        for hc in range(NHC):
            wg = wload_i32(pr, c, wgT, D, HID, hc * 128, 128, 0)
            wu = wload_i32(pr, c, wuT, D, HID, hc * 128, 128, NCH * 128)
            for tb in range(NSL):
                psG = P0
                psU = P1
                for kc in range(NCH):
                    pr.mm(psG[:, 0:SL], wg[kc],
                          XN[:, kc * T + tb * SL:kc * T + (tb + 1) * SL],
                          start=(kc == 0), stop=(kc == NCH - 1))
                for kc in range(NCH):
                    pr.mm(psU[:, 0:SL], wu[kc],
                          XN[:, kc * T + tb * SL:kc * T + (tb + 1) * SL],
                          start=(kc == 0), stop=(kc == NCH - 1))
                i = tb % 2
                gt = HT[:, i * SL:(i + 1) * SL]
                gs = HT[:, (2 + i) * SL:(3 + i) * SL]
                ut = HT[:, (4 + i) * SL:(5 + i) * SL]
                hb = HB[:, i * SL:(i + 1) * SL]
                pr.tss(gt, psG[:, 0:SL], PCOL[:, hc:hc + 1], ALU.mult)
                pr.act(gs, gt, AF.Silu)
                pr.tss(ut, psU[:, 0:SL], PCOL[:, NHC + hc:NHC + hc + 1],
                       ALU.mult)
                pr.tt(hb, gs, ut, ALU.mult)
                pr.dma("pool", HS.slab(T, hc, tb * SL, SL), hb)
        for tb in range(T // SLB):
            for hc in range(NHC):
                pr.dma("sp", HSS[:, hc * SLB:(hc + 1) * SLB],
                       HS.slab(T, hc, tb * SLB, SLB))
            for dd in range(NCH):
                wd = wload_i32(pr, c, wdT, HID, D, dd * 128, 128, 0)
                ps = P0 if dd % 2 == 0 else P1
                for hc in range(NHC):
                    pr.mm(ps[:, 0:SLB], wd[hc],
                          HSS[:, hc * SLB:(hc + 1) * SLB],
                          start=(hc == 0), stop=(hc == NHC - 1))
                i = dd % 2
                of = OF[:, i * SLB:(i + 1) * SLB]
                rs = RS[:, i * SLB:(i + 1) * SLB]
                pr.dma("sp", rs, rin.slab(T, dd, tb * SLB, SLB))
                pr.tss(of, ps[:, 0:SLB],
                       PCOL[:, 2 * NHC + dd:2 * NHC + dd + 1], ALU.mult)
                pr.tt(of, of, rs, ALU.add)
                pr.dma("pool", rout.slab(T, dd, tb * SLB, SLB), of)


def mk_sems(nc, st):
    sems = {}
    for e in ENG:
        sems[(e, False)] = st.enter_context(nc.semaphore("s_" + e))
        sems[(e, True)] = st.enter_context(nc.semaphore("q_" + e))
    return sems



def emit_l1attnb(pr, c, p, nc, rin, rout):
    pr.fence()
    wq = p.din("wqT", (D, D))
    wk = p.din("wkT", (D, D))
    wv = p.din("wvT", (D, D))
    wo = p.din("woT", (D, D))
    bq = p.din("bq", (D,))
    bk = p.din("bk", (D,))
    bv = p.din("bv", (D,))
    bo = p.din("obo", (D,))
    idn = p.din("idn", (128, 128), BF16)
    OUTS = p.dscratch("OUTS", (D, T), BF16)
    import contextlib
    with contextlib.ExitStack() as st:
        c["WF"] = sbuf(pr, nc, st, "WFb", 128, NCH * 128, F32)
        c["WB"] = sbuf(pr, nc, st, "WBb", 128, NCH * 128, BF16)
        XN = sbuf(pr, nc, st, "XNb", 128, NCH * T, BF16)
        QH = sbuf(pr, nc, st, "QHb", 128, T, BF16)
        KH = sbuf(pr, nc, st, "KHb", 128, T, BF16)
        VH = sbuf(pr, nc, st, "VHb", 128, T, BF16)
        VT = sbuf(pr, nc, st, "VTb", 128, T, BF16)
        AT = sbuf(pr, nc, st, "ATb", 128, T, BF16)
        SC = sbuf(pr, nc, st, "SCb", 128, T, F32)
        PP = sbuf(pr, nc, st, "PPb", 128, T, BF16)
        SM = sbuf(pr, nc, st, "SMb", 128, 8, F32)
        OB = sbuf(pr, nc, st, "OBb", 128, 2 * 128, BF16)
        OSL = sbuf(pr, nc, st, "OSLb", 128, NCH * SL, BF16)
        OF = sbuf(pr, nc, st, "OFb", 128, 2 * SL, F32)
        RS = sbuf(pr, nc, st, "RSb", 128, 2 * SL, F32)
        IDN = sbuf(pr, nc, st, "IDNb", 128, 128, BF16)
        PCOL = sbuf(pr, nc, st, "PCOLb", 128, 4 * NCH, F32)
        P0 = c["P0"]
        P1 = c["P1"]
        PT = c["PT"]
        print("l1attnb sbuf remaining:", nc.sbuf_bytes_remaining)

        pr.dma("sp", IDN[:, 0:128], idn.ap(0, [[128, 128], [1, 128]]))
        for i, v in enumerate([bq, bk, bv, bo]):
            load_pcols(p, pr, v, PCOL, i * NCH, NCH)
        emit_ln(pr, c, None, XN, 1e-5, src_dram_f32=rin)
        scale = 1.0 / math.sqrt(128.0)
        for h in range(NHEADS):
            for j, (w, dst) in enumerate([(wq, QH), (wk, KH), (wv, VH)]):
                wb = wload_f32(pr, c, w, D, D, h * 128, 128, 0)
                for tb in range(NSL):
                    ps = P0 if tb % 2 == 0 else P1
                    for kc in range(NCH):
                        pr.mm(ps[:, 0:SL], wb[kc],
                              XN[:, kc * T + tb * SL:kc * T + (tb + 1) * SL],
                              start=(kc == 0), stop=(kc == NCH - 1))
                    pr.tss(dst[:, tb * SL:(tb + 1) * SL], ps[:, 0:SL],
                           PCOL[:, j * NCH + h:j * NCH + h + 1], ALU.add)
            for i in range(NCH):
                pr.tp(PT[:, 0:128], VH[:, i * 128:(i + 1) * 128],
                      IDN[:, 0:128])
                pr.copy(VT[:, i * 128:(i + 1) * 128], PT[:, 0:128])
            for tq in range(NCH):
                q = QH[:, tq * 128:(tq + 1) * 128]
                for sb in range(NSL):
                    ps = P0 if sb % 2 == 0 else P1
                    pr.mm(ps[:, 0:SL], q, KH[:, sb * SL:(sb + 1) * SL],
                          start=True, stop=True)
                    pr.tss(SC[:, sb * SL:(sb + 1) * SL], ps[:, 0:SL],
                           scale, ALU.mult)
                m = SM[:, 0:1]
                nm = SM[:, 1:2]
                ssum = SM[:, 2:3]
                rr = SM[:, 3:4]
                pr.red(m, SC[:, 0:T], ALU.max)
                pr.tss(nm, m, -1.0, ALU.mult)
                pr.act(PP[:, 0:T], SC[:, 0:T], AF.Exp, bias=nm, accum=ssum)
                pr.recip(rr, ssum)
                pr.tss(PP[:, 0:T], PP[:, 0:T], rr, ALU.mult)
                for tkb in range(NCH):
                    pr.tp(PT[:, 0:128], PP[:, tkb * 128:(tkb + 1) * 128],
                          IDN[:, 0:128])
                    pr.copy(AT[:, tkb * 128:(tkb + 1) * 128], PT[:, 0:128])
                for tkb in range(NCH):
                    pr.mm(P0[:, 0:128], VT[:, tkb * 128:(tkb + 1) * 128],
                          AT[:, tkb * 128:(tkb + 1) * 128],
                          start=(tkb == 0), stop=(tkb == NCH - 1))
                ob = OB[:, (tq % 2) * 128:(tq % 2 + 1) * 128]
                pr.copy(ob, P0[:, 0:128])
                pr.dma("pool", OUTS.slab(T, h, tq * 128, 128), ob)
        for tb in range(NSL):
            for kc in range(NCH):
                pr.dma("sp", OSL[:, kc * SL:(kc + 1) * SL],
                       OUTS.slab(T, kc, tb * SL, SL))
            for oc in range(NCH):
                wb = wload_f32(pr, c, wo, D, D, oc * 128, 128, 0)
                ps = P0 if oc % 2 == 0 else P1
                for kc in range(NCH):
                    pr.mm(ps[:, 0:SL], wb[kc], OSL[:, kc * SL:(kc + 1) * SL],
                          start=(kc == 0), stop=(kc == NCH - 1))
                i = oc % 2
                of = OF[:, i * SL:(i + 1) * SL]
                rs = RS[:, i * SL:(i + 1) * SL]
                pr.dma("sp", rs, rin.slab(T, oc, tb * SL, SL))
                pr.tss(of, ps[:, 0:SL], PCOL[:, 3 * NCH + oc:3 * NCH + oc + 1],
                       ALU.add)
                pr.tt(of, of, rs, ALU.add)
                pr.dma("pool", rout.slab(T, oc, tb * SL, SL), of)



LAT = 512
RD = 64


def emit_l0attnb(pr, c, p, nc, rin, rout):
    pr.fence()
    wqd = p.din("wqd", (D, LAT))
    wqu = p.din("wqu", (LAT, D))
    wqrP = p.din("wqrP", (LAT, D))
    wkvd = p.din("wkvd", (D, LAT))
    wku = p.din("wku", (LAT, D))
    wvu = p.din("wvu", (LAT, D))
    wkrP = p.din("wkrP", (D, 128))
    wo = p.din("bwo", (D, D))
    idn = p.din("idn", (128, 128), BF16)
    prot = p.din("prot", (128, 128), BF16)
    costab = p.din("costab", (128, T))
    sintab = p.din("sintab", (128, T))
    maskt = p.din("maskt", (128, 128))
    OUTS = p.dscratch("OUTS", (D, T), BF16)
    import contextlib
    with contextlib.ExitStack() as st:
        c["WF"] = sbuf(pr, nc, st, "WFb", 128, NCH * 128, F32)
        c["WB"] = sbuf(pr, nc, st, "WBb", 128, NCH * 128, BF16)
        XN = sbuf(pr, nc, st, "XNb", 128, NCH * T, BF16)
        QL = sbuf(pr, nc, st, "QLb", 128, 4 * T, BF16)
        CK = sbuf(pr, nc, st, "CKb", 128, 4 * T, BF16)
        KR = sbuf(pr, nc, st, "KRb", 128, T, BF16)
        KRP = sbuf(pr, nc, st, "KRPb", 128, T, BF16)
        COS = sbuf(pr, nc, st, "COSb", 128, T, F32)
        SIN = sbuf(pr, nc, st, "SINb", 128, T, F32)
        RP = sbuf(pr, nc, st, "RPb", 128, 2 * SL, F32)
        SC = sbuf(pr, nc, st, "SCb", 128, T, F32)
        SM = sbuf(pr, nc, st, "SMb", 128, 8, F32)
        OB = sbuf(pr, nc, st, "OBb", 128, 2 * 128, BF16)
        OF = sbuf(pr, nc, st, "OFb", 128, 2 * SL, F32)
        RS = sbuf(pr, nc, st, "RSb", 128, 2 * SL, F32)
        IDN = sbuf(pr, nc, st, "IDNb", 128, 128, BF16)
        PRT = sbuf(pr, nc, st, "PRTb", 128, 128, BF16)
        MSK = sbuf(pr, nc, st, "MSKb", 128, 128, F32)
        P0 = c["P0"]
        P1 = c["P1"]
        PT = c["PT"]
        print("l0attnb sbuf remaining:", nc.sbuf_bytes_remaining)
        QH = lambda a, b: XN[:, 0 + a:0 + b]
        QRT = lambda a, b: XN[:, 2048 + a:2048 + b]
        QRP = lambda a, b: XN[:, 4096 + a:4096 + b]
        KH = lambda a, b: XN[:, 6144 + a:6144 + b]
        VH = lambda a, b: XN[:, 8192 + a:8192 + b]
        VT = lambda a, b: XN[:, 10240 + a:10240 + b]
        ATv = lambda a, b: XN[:, 12288 + a:12288 + b]
        PP = lambda a, b: XN[:, 14336 + a:14336 + b]
        OSL = lambda a, b: XN[:, 16384 + a:16384 + b]

        pr.dma("sp", IDN[:, 0:128], idn.ap(0, [[128, 128], [1, 128]]))
        pr.dma("sp", PRT[:, 0:128], prot.ap(0, [[128, 128], [1, 128]]))
        pr.dma("sp", MSK[:, 0:128], maskt.ap(0, [[128, 128], [1, 128]]))
        pr.dma("sp", COS[:, 0:T], costab.ap(0, [[T, 128], [1, T]]))
        pr.dma("sp", SIN[:, 0:T], sintab.ap(0, [[T, 128], [1, T]]))
        emit_ln(pr, c, None, XN, 1e-5, src_dram_f32=rin)

        def rope(dst, srct, tb):
            pr.mm(P1[:, 0:SL], PRT[:, 0:128], srct, start=True, stop=True)
            f1 = RP[:, 0:SL]
            f2 = RP[:, SL:2 * SL]
            pr.tt(f1, srct, COS[:, tb * SL:(tb + 1) * SL], ALU.mult)
            pr.tt(f2, P1[:, 0:SL], SIN[:, tb * SL:(tb + 1) * SL], ALU.mult)
            pr.tt(dst, f1, f2, ALU.add)

        for w, dst, M in [(wqd, QL, LAT), (wkvd, CK, LAT)]:
            for oc in range(M // 128):
                wb = wload_f32(pr, c, w, D, M, oc * 128, 128, 0)
                for tb in range(NSL):
                    ps = P0
                    for kc in range(NCH):
                        pr.mm(ps[:, 0:SL], wb[kc],
                              XN[:, kc * T + tb * SL:kc * T + (tb + 1) * SL],
                              start=(kc == 0), stop=(kc == NCH - 1))
                    pr.copy(dst[:, oc * T + tb * SL:oc * T + (tb + 1) * SL],
                            ps[:, 0:SL])
        wb = wload_f32(pr, c, wkrP, D, 128, 0, 128, 0)
        for tb in range(NSL):
            for kc in range(NCH):
                pr.mm(P0[:, 0:SL], wb[kc],
                      XN[:, kc * T + tb * SL:kc * T + (tb + 1) * SL],
                      start=(kc == 0), stop=(kc == NCH - 1))
            kt = KR[:, tb * SL:(tb + 1) * SL]
            pr.copy(kt, P0[:, 0:SL])
            rope(KRP[:, tb * SL:(tb + 1) * SL], kt, tb)

        scale = 1.0 / math.sqrt(128.0 + 64.0)
        for h in range(NHEADS):
            for w, dstf in [(wqu, QH), (wku, KH), (wvu, VH)]:
                wb = wload_f32(pr, c, w, LAT, D, h * 128, 128, 0)
                src = QL if w is wqu else CK
                for tb in range(NSL):
                    for kc in range(4):
                        pr.mm(P0[:, 0:SL], wb[kc],
                              src[:, kc * T + tb * SL:kc * T + (tb + 1) * SL],
                              start=(kc == 0), stop=(kc == 3))
                    pr.copy(dstf(tb * SL, (tb + 1) * SL), P0[:, 0:SL])
            wb = wload_f32(pr, c, wqrP, LAT, D, h * 128, 128, 0)
            for tb in range(NSL):
                for kc in range(4):
                    pr.mm(P0[:, 0:SL], wb[kc],
                          QL[:, kc * T + tb * SL:kc * T + (tb + 1) * SL],
                          start=(kc == 0), stop=(kc == 3))
                qt = QRT(tb * SL, (tb + 1) * SL)
                pr.copy(qt, P0[:, 0:SL])
                rope(QRP(tb * SL, (tb + 1) * SL), qt, tb)
            for i in range(NCH):
                pr.tp(PT[:, 0:128], VH(i * 128, (i + 1) * 128), IDN[:, 0:128])
                pr.copy(VT(i * 128, (i + 1) * 128), PT[:, 0:128])
            for tq in range(NCH):
                V = (tq + 1) * 128
                q = QH(tq * 128, (tq + 1) * 128)
                qr = QRP(tq * 128, (tq + 1) * 128)
                for c0 in range(0, V, SL):
                    c1 = min(V, c0 + SL)
                    n = c1 - c0
                    ps = P0
                    pr.mm(ps[:, 0:n], q, KH(c0, c1), start=True, stop=False)
                    pr.mm(ps[:, 0:n], qr, KRP[:, c0:c1], start=False,
                          stop=True)
                    pr.tss(SC[:, c0:c1], ps[:, 0:n], scale, ALU.mult)
                pr.tt(SC[:, V - 128:V], SC[:, V - 128:V], MSK[:, 0:128],
                      ALU.add)
                m = SM[:, 0:1]
                nm = SM[:, 1:2]
                ssum = SM[:, 2:3]
                rr = SM[:, 3:4]
                pr.red(m, SC[:, 0:V], ALU.max)
                pr.tss(nm, m, -1.0, ALU.mult)
                pr.act(PP(0, V), SC[:, 0:V], AF.Exp, bias=nm, accum=ssum)
                pr.recip(rr, ssum)
                pr.tss(PP(0, V), PP(0, V), rr, ALU.mult)
                for tkb in range(tq + 1):
                    pr.tp(PT[:, 0:128], PP(tkb * 128, (tkb + 1) * 128),
                          IDN[:, 0:128])
                    pr.copy(ATv(tkb * 128, (tkb + 1) * 128), PT[:, 0:128])
                for tkb in range(tq + 1):
                    pr.mm(P0[:, 0:128], VT(tkb * 128, (tkb + 1) * 128),
                          ATv(tkb * 128, (tkb + 1) * 128),
                          start=(tkb == 0), stop=(tkb == tq))
                ob = OB[:, (tq % 2) * 128:(tq % 2 + 1) * 128]
                pr.copy(ob, P0[:, 0:128])
                pr.dma("pool", OUTS.slab(T, h, tq * 128, 128), ob)
        for tb in range(NSL):
            for kc in range(NCH):
                pr.dma("sp", OSL(kc * SL, (kc + 1) * SL),
                       OUTS.slab(T, kc, tb * SL, SL))
            for oc in range(NCH):
                wb = wload_f32(pr, c, wo, D, D, oc * 128, 128, 0)
                ps = P0 if oc % 2 == 0 else P1
                for kc in range(NCH):
                    pr.mm(ps[:, 0:SL], wb[kc], OSL(kc * SL, (kc + 1) * SL),
                          start=(kc == 0), stop=(kc == NCH - 1))
                i = oc % 2
                of = OF[:, i * SL:(i + 1) * SL]
                rs = RS[:, i * SL:(i + 1) * SL]
                pr.dma("sp", rs, rin.slab(T, oc, tb * SL, SL))
                pr.tt(of, ps[:, 0:SL], rs, ALU.add)
                pr.dma("pool", rout.slab(T, oc, tb * SL, SL), of)


def l0attnb_tables():
    freqs = 1.0 / (10000.0 ** (np.arange(0, RD, 2, dtype=np.float64) / RD))
    tt = np.arange(T, dtype=np.float64)
    ang = np.outer(tt, freqs)
    cos = np.tile(np.cos(ang), (1, 2))
    sin = np.tile(np.sin(ang), (1, 2))
    costab = np.zeros((128, T), np.float32)
    sintab = np.zeros((128, T), np.float32)
    costab[0:RD, :] = cos.T
    sintab[0:RD, :] = sin.T
    prot = np.zeros((128, 128), np.float32)
    for i in range(32):
        prot[i + 32, i] = -1.0
    for i in range(32, RD):
        prot[i - 32, i] = 1.0
    maskt = np.where(np.arange(128)[None, :] <= np.arange(128)[:, None],
                     0.0, -1e9).astype(np.float32)
    return costab, sintab, prot.astype(BF), maskt



def _slopes(h):
    start = 2.0 ** (-(2.0 ** (-(math.log2(h) - 3))))
    return np.array([start ** (i + 1) for i in range(h)], dtype=np.float64)


def emit_l1attna(pr, c, p, nc, rin, rout):
    pr.fence()
    wqkvT = p.din("wqkvT", (D, 3 * D))
    wprT = p.din("wprT", (D, D))
    theta = p.din("theta", (NHEADS, 128))
    iotab = p.din("iotab", (128, T))
    ddt = p.din("ddt", (NHEADS, 128, 128))
    qdt = p.din("qdt", (NHEADS, 128, 128))
    kdt = p.din("kdt", (NHEADS, 128, 128))
    idn = p.din("idn", (128, 128), BF16)
    OUTS = p.dscratch("OUTS", (D, T), BF16)
    bdec = [float(np.exp(-_slopes(NHEADS)[h] * 128.0))
            for h in range(NHEADS)]
    import contextlib
    with contextlib.ExitStack() as st:
        c["WF"] = sbuf(pr, nc, st, "WFb", 128, NCH * 128, F32)
        c["WB"] = sbuf(pr, nc, st, "WBb", 128, NCH * 128, BF16)
        XN = sbuf(pr, nc, st, "XNb", 128, NCH * T, BF16)
        IOT = sbuf(pr, nc, st, "IOTb", 128, SL, F32)
        THB = sbuf(pr, nc, st, "THBb", 128, SL, F32)
        SN = sbuf(pr, nc, st, "SNb", 128, SL, F32)
        CSb = sbuf(pr, nc, st, "CSbb", 128, SL, F32)
        RRT = sbuf(pr, nc, st, "RRTb", 128, SL, F32)
        RRI = sbuf(pr, nc, st, "RRIb", 128, SL, I32)
        RRK = sbuf(pr, nc, st, "RRKb", 128, SL, F32)
        RRD = sbuf(pr, nc, st, "RRDb", 128, SL, F32)
        RRM = sbuf(pr, nc, st, "RRMb", 128, SL, F32)
        QR = sbuf(pr, nc, st, "QRb", 128, T, BF16)
        KRW = sbuf(pr, nc, st, "KRWb", 128, T, BF16)
        VH = sbuf(pr, nc, st, "VHb", 128, T, BF16)
        VT = sbuf(pr, nc, st, "VTb", 128, T, BF16)
        QE0 = sbuf(pr, nc, st, "QE0b", 128, T, BF16)
        QE1 = sbuf(pr, nc, st, "QE1b", 128, T, BF16)
        KE0 = sbuf(pr, nc, st, "KE0b", 128, T, BF16)
        KE1 = sbuf(pr, nc, st, "KE1b", 128, T, BF16)
        DDs = sbuf(pr, nc, st, "DDsb", 128, 128, F32)
        QDs = sbuf(pr, nc, st, "QDsb", 128, 128, F32)
        KDs = sbuf(pr, nc, st, "KDsb", 128, 128, F32)
        S0 = sbuf(pr, nc, st, "S0b", 128, 128, F32)
        S1 = sbuf(pr, nc, st, "S1b", 128, 128, F32)
        SB0 = sbuf(pr, nc, st, "SB0b", 128, 128, BF16)
        SB1 = sbuf(pr, nc, st, "SB1b", 128, 128, BF16)
        SDD = sbuf(pr, nc, st, "SDDb", 128, 128, BF16)
        AT0 = sbuf(pr, nc, st, "AT0b", 128, 128, BF16)
        QQ0 = sbuf(pr, nc, st, "QQ0b", 128, 128, BF16)
        QQ1 = sbuf(pr, nc, st, "QQ1b", 128, 128, BF16)
        KK0 = sbuf(pr, nc, st, "KK0b", 128, 128, BF16)
        KK1 = sbuf(pr, nc, st, "KK1b", 128, 128, BF16)
        KT0 = sbuf(pr, nc, st, "KT0b", 128, 128, BF16)
        KT1 = sbuf(pr, nc, st, "KT1b", 128, 128, BF16)
        THC = sbuf(pr, nc, st, "THCb", 128, 2, F32)
        OB = sbuf(pr, nc, st, "OBb", 128, 2 * 128, BF16)
        OF = sbuf(pr, nc, st, "OFb", 128, 2 * SL, F32)
        RS = sbuf(pr, nc, st, "RSb", 128, 2 * SL, F32)
        IDN = sbuf(pr, nc, st, "IDNb", 128, 128, BF16)
        P0 = c["P0"]
        P1 = c["P1"]
        PT = c["PT"]
        print("l1attna sbuf remaining:", nc.sbuf_bytes_remaining)
        OSL = lambda a, b: XN[:, 16384 + a:16384 + b]

        pr.dma("sp", IDN[:, 0:128], idn.ap(0, [[128, 128], [1, 128]]))
        emit_ln(pr, c, None, XN, 1e-5, src_dram_f32=rin)
        for h in range(NHEADS):
            pr.dma("sp", THC[:, 0:1], theta.ap(h * 128, [[1, 128], [1, 1]]))
            for w0, dst in [(0, QR), (D, KRW), (2 * D, VH)]:
                wb = wload_f32(pr, c, wqkvT, D, 3 * D, w0 + h * 128, 128, 0)
                for tb in range(NSL):
                    for kc in range(NCH):
                        pr.mm(P0[:, 0:SL], wb[kc],
                              XN[:, kc * T + tb * SL:kc * T + (tb + 1) * SL],
                              start=(kc == 0), stop=(kc == NCH - 1))
                    pr.copy(dst[:, tb * SL:(tb + 1) * SL], P0[:, 0:SL])
            for tb in range(NSL):
                t0, t1 = tb * SL, (tb + 1) * SL
                pr.dma("sp", IOT[:, 0:SL],
                       iotab.ap(t0, [[T, 128], [1, SL]]))
                pr.tss(THB[:, 0:SL], IOT[:, 0:SL], THC[:, 0:1], ALU.mult)

                def sin_of(dst, shift):
                    if shift:
                        pr.tss(dst[:, 0:SL], THB[:, 0:SL], shift, ALU.add)
                        src = dst
                    else:
                        src = THB
                    pr.tss(RRT[:, 0:SL], src[:, 0:SL], 1.0 / (2 * PI),
                           ALU.mult)
                    pr.copy(RRI[:, 0:SL], RRT[:, 0:SL])
                    pr.copy(RRK[:, 0:SL], RRI[:, 0:SL])
                    pr.stt(RRD[:, 0:SL], RRK[:, 0:SL], -1.0, RRT[:, 0:SL],
                           ALU.mult, ALU.add)
                    pr.tss(RRM[:, 0:SL], RRD[:, 0:SL], 0.0, ALU.is_lt)
                    pr.tt(RRD[:, 0:SL], RRD[:, 0:SL], RRM[:, 0:SL], ALU.add)
                    pr.act(dst[:, 0:SL], RRD[:, 0:SL], AF.Sin,
                           bias=ccol(pr, c, -PI), scale=ccol(pr, c, 2 * PI))
                    pr.tss(dst[:, 0:SL], dst[:, 0:SL], -1.0, ALU.mult)
                sin_of(SN, 0.0)
                sin_of(CSb, 0.5 * PI)
                pr.tt(QE0[:, t0:t1], QR[:, t0:t1], CSb[:, 0:SL], ALU.mult)
                pr.tt(QE1[:, t0:t1], QR[:, t0:t1], SN[:, 0:SL], ALU.mult)
                pr.tt(KE0[:, t0:t1], KRW[:, t0:t1], CSb[:, 0:SL], ALU.mult)
                pr.tt(KE1[:, t0:t1], KRW[:, t0:t1], SN[:, 0:SL], ALU.mult)
            for i in range(NCH):
                pr.tp(PT[:, 0:128], VH[:, i * 128:(i + 1) * 128],
                      IDN[:, 0:128])
                pr.copy(VT[:, i * 128:(i + 1) * 128], PT[:, 0:128])
            pr.dma("sp", DDs[:, 0:128],
                   ddt.ap(h * 128 * 128, [[128, 128], [1, 128]]))
            pr.dma("sp", QDs[:, 0:128],
                   qdt.ap(h * 128 * 128, [[128, 128], [1, 128]]))
            pr.dma("sp", KDs[:, 0:128],
                   kdt.ap(h * 128 * 128, [[128, 128], [1, 128]]))
            pr.memset(S0[:, 0:128], 0.0)
            pr.memset(S1[:, 0:128], 0.0)
            pr.memset(SB0[:, 0:128], 0.0)
            pr.memset(SB1[:, 0:128], 0.0)
            for i in range(NCH):
                b0, b1 = i * 128, (i + 1) * 128
                pr.mm(P0[:, 0:128], QE0[:, b0:b1], KE0[:, b0:b1],
                      start=True, stop=False)
                pr.mm(P0[:, 0:128], QE1[:, b0:b1], KE1[:, b0:b1],
                      start=False, stop=True)
                pr.tt(SDD[:, 0:128], P0[:, 0:128], DDs[:, 0:128], ALU.mult)
                pr.tp(PT[:, 0:128], SDD[:, 0:128], IDN[:, 0:128])
                pr.copy(AT0[:, 0:128], PT[:, 0:128])
                pr.tt(QQ0[:, 0:128], QE0[:, b0:b1], QDs[:, 0:128], ALU.mult)
                pr.tt(QQ1[:, 0:128], QE1[:, b0:b1], QDs[:, 0:128], ALU.mult)
                pr.mm(P1[:, 0:128], VT[:, b0:b1], AT0[:, 0:128],
                      start=True, stop=False)
                pr.mm(P1[:, 0:128], SB0[:, 0:128], QQ0[:, 0:128],
                      start=False, stop=False)
                pr.mm(P1[:, 0:128], SB1[:, 0:128], QQ1[:, 0:128],
                      start=False, stop=True)
                ob = OB[:, (i % 2) * 128:(i % 2 + 1) * 128]
                pr.copy(ob, P1[:, 0:128])
                pr.dma("pool", OUTS.slab(T, h, i * 128, 128), ob)
                pr.tt(KK0[:, 0:128], KE0[:, b0:b1], KDs[:, 0:128], ALU.mult)
                pr.tt(KK1[:, 0:128], KE1[:, b0:b1], KDs[:, 0:128], ALU.mult)
                pr.tp(PT[:, 0:128], KK0[:, 0:128], IDN[:, 0:128])
                pr.copy(KT0[:, 0:128], PT[:, 0:128])
                pr.tp(PT[:, 0:128], KK1[:, 0:128], IDN[:, 0:128])
                pr.copy(KT1[:, 0:128], PT[:, 0:128])
                pr.mm(P0[:, 0:128], KT0[:, 0:128], VT[:, b0:b1],
                      start=True, stop=True)
                pr.tss(S0[:, 0:128], S0[:, 0:128], bdec[h], ALU.mult)
                pr.tt(S0[:, 0:128], S0[:, 0:128], P0[:, 0:128], ALU.add)
                pr.copy(SB0[:, 0:128], S0[:, 0:128])
                pr.mm(P0[:, 0:128], KT1[:, 0:128], VT[:, b0:b1],
                      start=True, stop=True)
                pr.tss(S1[:, 0:128], S1[:, 0:128], bdec[h], ALU.mult)
                pr.tt(S1[:, 0:128], S1[:, 0:128], P0[:, 0:128], ALU.add)
                pr.copy(SB1[:, 0:128], S1[:, 0:128])
        for tb in range(NSL):
            for kc in range(NCH):
                pr.dma("sp", OSL(kc * SL, (kc + 1) * SL),
                       OUTS.slab(T, kc, tb * SL, SL))
            for oc in range(NCH):
                wb = wload_f32(pr, c, wprT, D, D, oc * 128, 128, 0)
                ps = P0 if oc % 2 == 0 else P1
                for kc in range(NCH):
                    pr.mm(ps[:, 0:SL], wb[kc], OSL(kc * SL, (kc + 1) * SL),
                          start=(kc == 0), stop=(kc == NCH - 1))
                i = oc % 2
                of = OF[:, i * SL:(i + 1) * SL]
                rs = RS[:, i * SL:(i + 1) * SL]
                pr.dma("sp", rs, rin.slab(T, oc, tb * SL, SL))
                pr.tt(of, ps[:, 0:SL], rs, ALU.add)
                pr.dma("pool", rout.slab(T, oc, tb * SL, SL), of)


def l1attna_tables():
    s = _slopes(NHEADS)
    iotab = np.broadcast_to(np.arange(T, dtype=np.float32),
                            (128, T)).copy()
    j = np.arange(128, dtype=np.float64)
    diff = j[:, None] - j[None, :]
    ddt = np.where(diff >= 0, np.exp(-s[:, None, None] * diff[None]),
                   0.0).astype(np.float32)
    qdt = np.broadcast_to(np.exp(-s[:, None] * (j + 1.0))[:, None, :],
                          (NHEADS, 128, 128)).astype(np.float32).copy()
    kdt = np.broadcast_to(np.exp(-s[:, None] * (127.0 - j))[:, None, :],
                          (NHEADS, 128, 128)).astype(np.float32).copy()
    return iotab, ddt, qdt, kdt



HID0 = 8192
NHC0 = HID0 // 128


def emit_l0attna(pr, c, p, nc, rin, rout):
    pr.fence()
    wv = p.din("wv", (D, D))
    wr = p.din("wr", (D, D))
    wo = p.din("wo", (D, D))
    wcr = p.din("wcr", (D, D))
    wck = p.din("wck", (D, HID0))
    wcv = p.din("wcv", (HID0, D))
    vecs = {}
    for nm in ["ln1g", "ln1b", "ln2g", "ln2b", "tmg", "tmb", "cmg", "cmb",
               "td", "tf", "bo"]:
        vecs[nm] = p.din(nm, (D,))
    revb = p.din("revb", (128, T))
    X0S = p.dscratch("X0S", (D, T))
    X2S = p.dscratch("X2S", (D, T))
    KCS = p.dscratch("KCS", (HID0, T), BF16)
    import contextlib
    with contextlib.ExitStack() as st:
        c["WF"] = sbuf(pr, nc, st, "WFb", 128, NCH * 128, F32)
        c["WB"] = sbuf(pr, nc, st, "WBb", 128, 2 * NCH * 128, BF16)
        A = sbuf(pr, nc, st, "Ab", 128, NCH * T, BF16)
        B = sbuf(pr, nc, st, "Bb", 128, NCH * T, BF16)
        MISC = sbuf(pr, nc, st, "MISCb", 128, 8 * SL, F32)
        VMP = sbuf(pr, nc, st, "VMPb", 128, 64, F32)
        VMX = sbuf(pr, nc, st, "VMXb", 128, 16, F32)
        VMXB = sbuf(pr, nc, st, "VMXBb", 128, 16, BF16)
        Y0 = sbuf(pr, nc, st, "Y0b", 128, 16, F32)
        RCB = sbuf(pr, nc, st, "RCBb", 128, SL, BF16)
        KCB = sbuf(pr, nc, st, "KCBb", 128, SL, BF16)
        PCOL = sbuf(pr, nc, st, "PCOLb", 128, 12 * NCH, F32)
        P0 = c["P0"]
        P1 = c["P1"]
        PT = c["PT"]
        print("l0attna sbuf remaining:", nc.sbuf_bytes_remaining)

        order = ["ln1g", "ln1b", "ln2g", "ln2b", "tmg", "tmb", "cmg", "cmb",
                 "td", "tf", "bo"]
        for i, nm in enumerate(order):
            load_pcols(p, pr, vecs[nm], PCOL, i * NCH, NCH)
        cb = {nm: i * NCH for i, nm in enumerate(order)}
        cb["tf1m"] = 11 * NCH
        pr.ts(PCOL[:, cb["tf1m"]:cb["tf1m"] + NCH],
              PCOL[:, cb["tf"]:cb["tf"] + NCH], -1.0, 1.0, ALU.mult, ALU.add)

        def pcv(nm, ch):
            return PCOL[:, cb[nm] + ch:cb[nm] + ch + 1]

        emit_ln(pr, c, None, None, 1e-5, src_dram_f32=rin, dst_dram=X0S)
        emit_ln(pr, c, None, A, 1e-12, g=(PCOL, cb["ln1g"]),
                b=(PCOL, cb["ln1b"]), src_dram_f32=X0S)
        emit_ln(pr, c, A, B, 1e-12, g=(PCOL, cb["tmg"]), b=(PCOL, cb["tmb"]))
        for ch in range(NCH):
            for tb in range(NSL):
                o0 = ch * T + tb * SL
                pr.tss(A[:, o0:o0 + SL], B[:, o0:o0 + SL], pcv("tf", ch),
                       ALU.mult)
                if tb == 0:
                    pr.stt(A[:, o0 + 1:o0 + SL], B[:, o0:o0 + SL - 1],
                           pcv("tf1m", ch), A[:, o0 + 1:o0 + SL],
                           ALU.mult, ALU.add)
                else:
                    pr.stt(A[:, o0:o0 + SL], B[:, o0 - 1:o0 + SL - 1],
                           pcv("tf1m", ch), A[:, o0:o0 + SL],
                           ALU.mult, ALU.add)
        for oc in range(NCH):
            wb = wload_f32(pr, c, wv, D, D, oc * 128, 128, 0)
            for tb in range(NSL):
                for kc in range(NCH):
                    pr.mm(P0[:, 0:SL], wb[kc],
                          B[:, kc * T + tb * SL:kc * T + (tb + 1) * SL],
                          start=(kc == 0), stop=(kc == NCH - 1))
                rv = MISC[:, 0:SL]
                d1 = MISC[:, SL:2 * SL]
                d2 = MISC[:, 2 * SL:3 * SL]
                vd = MISC[:, 3 * SL:4 * SL]
                pr.dma("sp", rv, revb.ap(tb * SL, [[T, 128], [1, SL]]))
                pr.tss(d1, rv, pcv("td", oc), ALU.mult)
                pr.act(d2, d1, AF.Exp)
                pr.tt(vd, P0[:, 0:SL], d2, ALU.mult)
                pr.red(VMP[:, oc * NSL + tb:oc * NSL + tb + 1], vd, ALU.add)
        for oc in range(NCH):
            pr.red(VMX[:, oc:oc + 1], VMP[:, oc * NSL:(oc + 1) * NSL],
                   ALU.add)
        pr.copy(VMXB[:, 0:NCH], VMX[:, 0:NCH])
        for oc in range(NCH):
            wb = wload_f32(pr, c, wo, D, D, oc * 128, 128, 0)
            for kc in range(NCH):
                pr.mm(P1[:, 0:1], wb[kc], VMXB[:, kc:kc + 1],
                      start=(kc == 0), stop=(kc == NCH - 1))
            pr.tt(Y0[:, oc:oc + 1], P1[:, 0:1], pcv("bo", oc), ALU.add)
        for oc in range(NCH):
            wb = wload_f32(pr, c, wr, D, D, oc * 128, 128, 0)
            for tb in range(NSL):
                for kc in range(NCH):
                    pr.mm(P0[:, 0:SL], wb[kc],
                          A[:, kc * T + tb * SL:kc * T + (tb + 1) * SL],
                          start=(kc == 0), stop=(kc == NCH - 1))
                sg = MISC[:, SL:2 * SL]
                x2 = MISC[:, 2 * SL:3 * SL]
                xr = MISC[:, 3 * SL:4 * SL]
                pr.act(sg, P0[:, 0:SL], AF.Sigmoid)
                pr.dma("sp", xr, X0S.slab(T, oc, tb * SL, SL))
                pr.stt(x2, sg, Y0[:, oc:oc + 1], xr, ALU.mult, ALU.add)
                pr.dma("pool", X2S.slab(T, oc, tb * SL, SL), x2)
        emit_ln(pr, c, None, None, 1e-12, g=(PCOL, cb["ln2g"]),
                b=(PCOL, cb["ln2b"]), src_dram_f32=X2S, dst_dram=X0S)
        emit_ln(pr, c, None, A, 1e-12, g=(PCOL, cb["cmg"]),
                b=(PCOL, cb["cmb"]), src_dram_f32=X0S)
        for hc in range(NHC0):
            wb = wload_f32(pr, c, wck, D, HID0, hc * 128, 128, 0)
            for tb in range(NSL):
                if tb == 0:
                    lo, n, po = 0, SL - 1, 1
                else:
                    lo, n, po = tb * SL - 1, SL, 0
                for kc in range(NCH):
                    pr.mm(P0[:, po:po + n], wb[kc],
                          A[:, kc * T + lo:kc * T + lo + n],
                          start=(kc == 0), stop=(kc == NCH - 1))
                kf = MISC[:, 7 * SL + po:7 * SL + po + n]
                pr.act(kf, P0[:, po:po + n], AF.Relu)
                pr.act(KCB[:, po:po + n], kf, AF.Square)
                if tb == 0:
                    pr.memset(KCB[:, 0:1], 0.0)
                pr.dma("pool", KCS.slab(T, hc, tb * SL, SL), KCB[:, 0:SL])
        for tb in range(NSL):
            for hc in range(NHC0):
                pr.dma("sp", B[:, hc * SL:(hc + 1) * SL],
                       KCS.slab(T, hc, tb * SL, SL))
            for dd in range(NCH):
                wb = wload_f32(pr, c, wcr, D, D, dd * 128, 128, 0)
                for kc in range(NCH):
                    pr.mm(P1[:, 0:SL], wb[kc],
                          A[:, kc * T + tb * SL:kc * T + (tb + 1) * SL],
                          start=(kc == 0), stop=(kc == NCH - 1))
                pr.act(RCB[:, 0:SL], P1[:, 0:SL], AF.Sigmoid)
                for half in range(2):
                    for q in range(2):
                        k0 = half * 32 + q * 16
                        wbv = wload_f32(pr, c, wcv, HID0, D, dd * 128, 128,
                                        q * NCH * 128, k0=k0, kn=16)
                        for kk in range(16):
                            hc = k0 + kk
                            pr.mm(P0[:, 0:SL], wbv[kk],
                                  B[:, hc * SL:(hc + 1) * SL],
                                  start=(hc == 0), stop=(hc == NHC0 - 1))
                of = MISC[:, 4 * SL:5 * SL]
                rs = MISC[:, 5 * SL:6 * SL]
                x2r = MISC[:, 6 * SL:7 * SL]
                pr.dma("sp", rs, rin.slab(T, dd, tb * SL, SL))
                pr.dma("sp", x2r, X2S.slab(T, dd, tb * SL, SL))
                pr.tt(of, P0[:, 0:SL], RCB[:, 0:SL], ALU.mult)
                pr.tt(of, of, x2r, ALU.add)
                pr.tt(of, of, rs, ALU.add)
                pr.dma("pool", rout.slab(T, dd, tb * SL, SL), of)




def _mk_base(p, st, nc):
    pr = Pr(nc)
    c = {}
    common_consts(pr, c, nc, st)
    ln_bufs(pr, c, nc, st)
    c["P0"] = psum(nc, st, "P0", 128, SL, F32)
    c["P1"] = psum(nc, st, "P1", 128, SL, F32)
    c["PT"] = psum(nc, st, "PTp", 128, 512, BF16)
    return pr, c


def _standalone(name, emitfn):
    p = Prog(name)
    nc = p.nc
    rin = p.din("rin", (D, T))
    rout = p.dout("rout", (D, T))
    import contextlib
    with contextlib.ExitStack() as st:
        pr, c = _mk_base(p, st, nc)
        block = st.enter_context(nc.Block())
        emitfn(pr, c, p, nc, rin, rout)
        pr.finish(block, mk_sems(nc, st))
    return nc


def build_l1ffn():
    return _standalone("l1ffn", emit_l1ffn)


def build_l0ffn():
    return _standalone("l0ffn", emit_l0ffn)


def build_l1attnb():
    return _standalone("l1attnb", emit_l1attnb)


def build_l0attnb():
    return _standalone("l0attnb", emit_l0attnb)


def build_l1attna():
    return _standalone("l1attna", emit_l1attna)


def build_l0attna():
    return _standalone("l0attna", emit_l0attna)


def build_fused():
    p = Prog("fused")
    nc = p.nc
    rin = p.din("rin", (D, T))
    rout = p.dout("rout", (D, T))
    Rs = [p.dscratch(f"RESID{i}", (D, T)) for i in range(7)]
    import contextlib
    with contextlib.ExitStack() as st:
        pr, c = _mk_base(p, st, nc)
        block = st.enter_context(nc.Block())
        chain = [emit_l0attna, emit_l0ffn, emit_l0attnb, emit_l0ffn,
                 emit_l1attna, emit_l1ffn, emit_l1attnb, emit_l1ffn]
        srcs = [rin] + Rs
        dsts = Rs + [rout]
        for f, a, b in zip(chain, srcs, dsts):
            f(pr, c, p, nc, a, b)
        pr.finish(block, mk_sems(nc, st))
    return nc



_progs = {}


def get_prog(name, builder):
    if name not in _progs:
        _progs[name] = builder()
    return _progs[name]


def run8(nc, per_core, shared):
    in_maps = []
    for i in range(8):
        m = dict(shared)
        for k, v in per_core.items():
            m[k] = v[i]
        in_maps.append(m)
    res = run_bass_kernel_spmd(nc, in_maps, core_ids=list(range(8)))
    return res.results


def _f32c(x):
    return np.ascontiguousarray(x, dtype=np.float32)


def kernel(**inp):
    x = inp["L0_attn_a_x"]
    resid = np.ascontiguousarray(np.transpose(np.asarray(x, np.float32),
                                              (0, 2, 1)))

    def launch(name, builder, r, shared):
        nc = get_prog(name, builder)
        res = run8(nc, {"rin": r}, shared)
        return np.stack([res[i]["rout"] for i in range(8)])

    revb = np.broadcast_to((T - 1.0 - np.arange(T)).astype(np.float32),
                           (128, T)).copy()
    sh_l0a = {"wv": _f32c(inp["L0_attn_a_Wv"]), "wr": _f32c(inp["L0_attn_a_Wr"]),
              "wo": _f32c(inp["L0_attn_a_Wo"]),
              "wcr": _f32c(inp["L0_attn_a_Wcr"]),
              "wck": _f32c(inp["L0_attn_a_Wck"]),
              "wcv": _f32c(inp["L0_attn_a_Wcv"]),
              "ln1g": _f32c(inp["L0_attn_a_ln1_g"]),
              "ln1b": _f32c(inp["L0_attn_a_ln1_b"]),
              "ln2g": _f32c(inp["L0_attn_a_ln2_g"]),
              "ln2b": _f32c(inp["L0_attn_a_ln2_b"]),
              "tmg": _f32c(inp["L0_attn_a_tm_g"]),
              "tmb": _f32c(inp["L0_attn_a_tm_b"]),
              "cmg": _f32c(inp["L0_attn_a_cm_g"]),
              "cmb": _f32c(inp["L0_attn_a_cm_b"]),
              "td": _f32c(inp["L0_attn_a_time_decay"]),
              "tf": _f32c(inp["L0_attn_a_time_first"]),
              "bo": _f32c(inp["L0_attn_a_bo"]), "revb": revb}
    sh_l0f = {"wgT": np.ascontiguousarray(np.asarray(inp["L0_ffn_wq_gate"]).T),
              "wuT": np.ascontiguousarray(np.asarray(inp["L0_ffn_wq_up"]).T),
              "wdT": np.ascontiguousarray(np.asarray(inp["L0_ffn_wq_down"]).T),
              "sg": _f32c(inp["L0_ffn_s_gate"]), "su": _f32c(inp["L0_ffn_s_up"]),
              "sd": _f32c(inp["L0_ffn_s_down"])}
    costab, sintab, prot, maskt = l0attnb_tables()
    idn = np.eye(128, dtype=np.float32).astype(BF)
    wqrP = np.zeros((LAT, D), np.float32)
    for h in range(NHEADS):
        wqrP[:, h * 128:h * 128 + RD] = \
            np.asarray(inp["L0_attn_b_Wqr"])[:, h * RD:(h + 1) * RD]
    wkrP = np.zeros((D, 128), np.float32)
    wkrP[:, 0:RD] = np.asarray(inp["L0_attn_b_Wkr"])
    sh_l0b = {"wqd": _f32c(inp["L0_attn_b_Wqd"]),
              "wqu": _f32c(inp["L0_attn_b_Wqu"]), "wqrP": wqrP,
              "wkvd": _f32c(inp["L0_attn_b_Wkvd"]),
              "wku": _f32c(inp["L0_attn_b_Wku"]),
              "wvu": _f32c(inp["L0_attn_b_Wvu"]), "wkrP": wkrP,
              "bwo": _f32c(inp["L0_attn_b_Wo"]), "idn": idn, "prot": prot,
              "costab": costab, "sintab": sintab, "maskt": maskt}
    iotab, ddt, qdt, kdt = l1attna_tables()
    sh_l1a = {"wqkvT": _f32c(np.asarray(inp["L1_attn_a_w_qkv"]).T),
              "wprT": _f32c(np.asarray(inp["L1_attn_a_w_proj"]).T),
              "theta": _f32c(np.asarray(inp["L1_attn_a_theta"]).reshape(16, 128)),
              "iotab": iotab, "ddt": ddt, "qdt": qdt, "kdt": kdt, "idn": idn}
    sh_l1f = {"wupT": _f32c(np.asarray(inp["L1_ffn_W_up"]).T),
              "wdnT": _f32c(np.asarray(inp["L1_ffn_W_down"]).T),
              "wex": _f32c(inp["L1_ffn_W_expert"])}
    sh_l1b = {"wqT": _f32c(np.asarray(inp["L1_attn_b_Wq"]).T),
              "wkT": _f32c(np.asarray(inp["L1_attn_b_Wk"]).T),
              "wvT": _f32c(np.asarray(inp["L1_attn_b_Wv"]).T),
              "woT": _f32c(np.asarray(inp["L1_attn_b_Wo"]).T),
              "bq": _f32c(inp["L1_attn_b_bq"]), "bk": _f32c(inp["L1_attn_b_bk"]),
              "bv": _f32c(inp["L1_attn_b_bv"]), "obo": _f32c(inp["L1_attn_b_bo"]),
              "idn": idn}

    if FUSED:
        shared = {}
        for d in (sh_l0a, sh_l0f, sh_l0b, sh_l1a, sh_l1f, sh_l1b):
            shared.update(d)
        resid = launch("fused", build_fused, resid, shared)
    else:
        resid = launch("l0attna", build_l0attna, resid, sh_l0a)
        resid = launch("l0ffn", build_l0ffn, resid, sh_l0f)
        resid = launch("l0attnb", build_l0attnb, resid, sh_l0b)
        resid = launch("l0ffn", build_l0ffn, resid, sh_l0f)
        resid = launch("l1attna", build_l1attna, resid, sh_l1a)
        resid = launch("l1ffn", build_l1ffn, resid, sh_l1f)
        resid = launch("l1attnb", build_l1attnb, resid, sh_l1b)
        resid = launch("l1ffn", build_l1ffn, resid, sh_l1f)
    out = np.ascontiguousarray(np.transpose(resid, (0, 2, 1)))
    return out
